# version 36
# baseline (speedup 1.0000x reference)
import numpy as np
import ml_dtypes
from contextlib import ExitStack

import concourse.bass as bass
import concourse.mybir as mybir
from concourse.bass_utils import run_bass_kernel_spmd

F32 = mybir.dt.float32
BF16 = mybir.dt.bfloat16
AF = mybir.ActivationFunctionType
ALU = mybir.AluOpType

T = 2048
D = 1024
NCH = 8
H = 16
QBW = 512
NQB = T // QBW
JT = 128
NJT = T // JT
VW = 66
VROW = H * VW
EPS = 1e-5
EXP_SHIFT = -16.0
SKIP_SAME_ENGINE = True


class _Eng:
    def __init__(self, name, step):
        self.name = name
        self.step = step
        self.val = 0
        self.items = []
        self.watermark = {}


class _Res:
    __slots__ = ("w", "r")

    def __init__(self):
        self.w = None
        self.r = set()


_COMPUTE = {"tensor", "vector", "scalar"}


class _B:

    def __init__(self):
        self.engs = {n: _Eng(n, 16 if n in ("sync", "gpsimd") else 1)
                     for n in ("sync", "gpsimd", "tensor", "vector", "scalar")}
        self.streams = {}
        self.stream_names = []
        self.ambient = []
        self.buf = None
        self.nodes = []

    def _stream(self, res):
        st = self.streams.get(id(res))
        if st is None:
            name = f"dst{len(self.stream_names)}"
            self.stream_names.append(name)
            st = self.streams[id(res)] = [name, 0]
        return st

    def emit(self, engname, fn, reads=(), writes=(), cost=600):
        if self.buf is not None:
            self.buf.append((engname, fn, list(reads), list(writes), cost))
            return
        nid = len(self.nodes)
        deps = set()

        def dep(d):
            if d is not None and d != nid:
                deps.add(d)

        for res in list(reads) + self.ambient:
            dep(res.w)
        for res in writes:
            dep(res.w)
            for d in res.r:
                dep(d)
        self.nodes.append([engname, fn, list(reads), list(writes), cost,
                           sorted(deps)])
        for res in reads:
            res.r.add(nid)
        for res in writes:
            res.w = nid
            res.r = set()

    def schedule(self):
        n = len(self.nodes)
        nodes = self.nodes
        ndeps = [len(nd[5]) for nd in nodes]
        out_edges = [[] for _ in range(n)]
        for i, nd in enumerate(nodes):
            for d in nd[5]:
                out_edges[d].append(i)
        finish = [0.0] * n
        clock = {e: 0.0 for e in self.engs}
        import heapq
        ready_heap = []

        def est_start(i):
            e = nodes[i][0]
            t = clock[e]
            for d in nodes[i][5]:
                f = finish[d] + (450.0 if nodes[d][0] != e else 0.0)
                if f > t:
                    t = f
            return t

        cp = [0.0] * n
        for i in range(n - 1, -1, -1):
            m = 0.0
            for j in out_edges[i]:
                if cp[j] > m:
                    m = cp[j]
            cp[i] = m + nodes[i][4]

        pend_cnt = ndeps[:]
        avail = [i for i in range(n) if pend_cnt[i] == 0]
        order = []
        SLACK = 0.0
        while avail or ready_heap:
            best = None
            best_t = None
            for i in avail:
                t = est_start(i)
                if best is None or t < best_t - 1e-9:
                    best, best_t = i, t
            cand, cand_key = None, None
            for i in avail:
                t = est_start(i)
                if t <= best_t + SLACK:
                    k2 = (-cp[i], i)
                    if cand is None or k2 < cand_key:
                        cand, cand_key = i, k2
            i = cand
            best_t = est_start(i)
            avail.remove(i)
            e = nodes[i][0]
            start = best_t
            finish[i] = start + nodes[i][4]
            clock[e] = finish[i]
            order.append(i)
            for j in out_edges[i]:
                pend_cnt[j] -= 1
                if pend_cnt[j] == 0:
                    avail.append(j)
        assert len(order) == n
        return order, finish

    def finalize(self, order):
        for nd in self.nodes:
            nd.append(None)
        last_w = {}
        readers = {}
        wl_all = []
        eng_items = {e: [] for e in self.engs}
        watermark = {e: {} for e in self.engs}
        vals = {e: 0 for e in self.engs}
        pos = {}
        for k, i in enumerate(order):
            pos[i] = k
        res_w = {}
        res_r = {}

        def key(res):
            return id(res)

        for i in order:
            engname, fn, reads, writes, cost, deps = self.nodes[i][:6]
            eng = self.engs[engname]
            is_dma = engname not in _COMPUTE
            waits = {}

            def need(dep_sem, val):
                if dep_sem == engname and not is_dma and engname != "scalar":
                    return
                if val > waits.get(dep_sem, -1):
                    waits[dep_sem] = val

            for res in list(reads) + self.ambient:
                w = res_w.get(key(res))
                if w is not None:
                    need(*w)
            for res in writes:
                w = res_w.get(key(res))
                if w is not None:
                    need(*w)
                for en, v in res_r.get(key(res), {}).items():
                    need(en, v)
            wl = []
            for en, v in sorted(waits.items()):
                if v > watermark[engname].get(en, -1):
                    watermark[engname][en] = v
                    wl.append((en, v))
            if is_dma:
                assert len(writes) == 1
                st = self._stream(writes[0])
                st[1] += 16
                sem_name, val = st[0], st[1]
            else:
                vals[engname] += 1
                sem_name, val = engname, vals[engname]
            eng_items[engname].append((wl, fn, (sem_name, val)))
            for res in reads:
                res_r.setdefault(key(res), {})[sem_name] = val
            for res in writes:
                res_w[key(res)] = (sem_name, val)
                res_r[key(res)] = {}
        for e, items in eng_items.items():
            self.engs[e].items = items


def build_program():
    nc = bass.Bass("TRN2", target_bir_lowering=False,
                   detect_race_conditions=not SKIP_SAME_ENGINE)

    xT = nc.dram_tensor("xT", [D, T], F32, kind="ExternalInput")
    wqk = nc.dram_tensor("wqk", [16, 128, 1024], F32, kind="ExternalInput")
    wv = nc.dram_tensor("wv", [2, 128, 4096], F32, kind="ExternalInput")
    wr = nc.dram_tensor("wr", [8, 128, 1024], F32, kind="ExternalInput")
    wk2 = nc.dram_tensor("wk2", [8, 128, 1024], F32, kind="ExternalInput")
    wv2 = nc.dram_tensor("wv2", [8, 128, 1024], F32, kind="ExternalInput")
    mur = nc.dram_tensor("mur", [128, 8], F32, kind="ExternalInput")
    muk = nc.dram_tensor("muk", [128, 8], F32, kind="ExternalInput")
    yT = nc.dram_tensor("yT", [D, T], F32, kind="ExternalOutput")
    wrb_d = nc.dram_tensor("wrb_d", [8, 128, 1024], BF16)
    wv2b_d = nc.dram_tensor("wv2b_d", [8, 128, 1024], BF16)
    wk2b_d = nc.dram_tensor("wk2b_d", [8, 128, 1024], BF16)
    bcscr_d = nc.dram_tensor("bcscr_d", [4, 1024], F32)

    tri_np = np.triu(np.ones((128, 128), dtype=np.float32)).astype(ml_dtypes.bfloat16)
    tri_c = nc.inline_tensor(tri_np, name="tri_c")

    b = _B()
    AP = bass.AP

    with ExitStack() as ctx:
        def sb(name, shape, dtype):
            return ctx.enter_context(nc.sbuf_tensor(name, shape, dtype))

        q_sb = sb("q_sb", [128, 16384], BF16)
        k_sb = sb("k_sb", [128, 16384], BF16)
        v_sb = sb("v_sb", [128, VROW * NJT], BF16)
        x1d = sb("x1d", [128, 8192], F32)
        probs = sb("probs", [128, 4096], BF16)
        rowa = sb("rowa", [128, 2048], F32)
        bc_sb = sb("bc_sb", [128, 1024], F32)
        xres = sb("xres", [128, 1024], F32)
        halo = sb("halo", [128, 16], F32)
        ones_f = sb("ones_f", [128, 128], F32)
        ones_b = sb("ones_b", [128, 1], BF16)
        mur_sb = sb("mur_sb", [128, 8], F32)
        muk_sb = sb("muk_sb", [128, 8], F32)
        omur_sb = sb("omur_sb", [128, 8], F32)
        omuk_sb = sb("omuk_sb", [128, 8], F32)
        tri_sb = sb("tri_sb", [128, 128], BF16)
        neg16 = sb("neg16", [128, 1], F32)
        epsc = sb("epsc", [128, 1], F32)

        ps = ctx.enter_context(nc.psum_tensor("ps", [128, 4096], F32))

        def bank(i):
            return ps[:, i * 512:(i + 1) * 512]

        r_q = [_Res() for _ in range(NCH)]
        r_k = [_Res() for _ in range(NCH)]
        r_v = [_Res() for _ in range(NJT)]
        r_x1 = [[[_Res(), _Res()] for _ in range(NCH)] for _ in range(2)]
        r_probs = [_Res() for _ in range(4)]
        r_rowa = [_Res() for _ in range(4)]
        r_rowb = [[_Res() for _ in range(4)] for _ in range(2)]
        r_xres = [_Res() for _ in range(2)]
        r_halo = [_Res() for _ in range(2)]
        r_const = _Res()
        r_pb = [_Res() for _ in range(8)]
        r_bcs = [_Res() for _ in range(2)]
        r_lnbc = _Res()
        r_scr = [_Res() for _ in range(4)]

        def init_v(eng):
            eng.memset(ones_f[:, :], 1.0)
            eng.memset(ones_b[:, :], 1.0)
            eng.memset(neg16[:, :], EXP_SHIFT)
            eng.memset(epsc[:, :], EPS)
            eng.memset(halo[:, :], 0.0)
            return eng.memset(AP(v_sb, 64, [[VROW * NJT, 128], [VW, H * NJT]]), 1.0)
        b.emit("vector", init_v, writes=[r_const, r_halo[0], r_halo[1]] + r_v)

        b.emit("sync", lambda e: e.dma_start(out=tri_sb[:, :], in_=tri_c[:, :]),
               writes=[r_const])
        b.emit("sync", lambda e: e.dma_start(out=mur_sb[:, :], in_=mur[:, :]),
               writes=[r_const])
        b.emit("sync", lambda e: e.dma_start(out=muk_sb[:, :], in_=muk[:, :]),
               writes=[r_const])

        def init_omu(eng):
            eng.tensor_scalar(omur_sb[:, :], mur_sb[:, :], -1.0, 1.0, ALU.mult, ALU.add)
            return eng.tensor_scalar(omuk_sb[:, :], muk_sb[:, :], -1.0, 1.0,
                                     ALU.mult, ALU.add)
        b.emit("vector", init_omu, reads=[r_const], writes=[r_const])

        pend = []

        def x1h(qb):
            return (qb % 2) * 4096

        def attn_pair(qb, c, filler=None):
            njc = 4 * qb + 4
            final_tail = (qb == NQB - 1 and c == NCH - 1)
            xs = c % 2
            b.emit("gpsimd", lambda e, c=c, qb=qb, xs=xs: e.dma_start(
                out=xres[:, xs * 512:(xs + 1) * 512],
                in_=AP(xT, c * 128 * 2048 + qb * 512, [[2048, 128], [1, 512]])),
                writes=[r_xres[xs]], cost=1300)

            def sc_mms(eng, c, qb, jc):
                sbk = 2 * (jc % 2)
                eng.matmul(
                    bank(sbk),
                    AP(k_sb, c * 2048 + jc * 128, [[16384, 64], [1, 128]]),
                    AP(q_sb, c * 2048 + qb * 512, [[16384, 64], [1, 512]]),
                    start=True, stop=True)
                return eng.matmul(
                    bank(sbk + 1),
                    AP(k_sb, 64 * 16384 + c * 2048 + jc * 128, [[16384, 64], [1, 128]]),
                    AP(q_sb, 64 * 16384 + c * 2048 + qb * 512, [[16384, 64], [1, 512]]),
                    start=True, stop=True, tile_position=(64, 0))

            def av_mms(eng, c, jc, par, first, last, qb=qb):
                off = max(jc - 4 * qb, 0) * 128
                eng.matmul(
                    ps[0:65, 4 * 512 + off:5 * 512],
                    AP(v_sb, jc * VROW + (2 * c) * VW, [[VROW * NJT, 128], [1, 65]]),
                    AP(probs, par * 1024 + off, [[4096, 128], [1, 512 - off]]),
                    start=first, stop=last)
                return eng.matmul(
                    ps[0:65, 5 * 512 + off:6 * 512],
                    AP(v_sb, jc * VROW + (2 * c + 1) * VW, [[VROW * NJT, 128], [1, 65]]),
                    AP(probs, par * 1024 + 512 + off, [[4096, 128], [1, 512 - off]]),
                    start=first, stop=last)

            def emit_sc(jc):
                sbk = 2 * (jc % 2)
                b.emit("tensor", lambda e, c=c, qb=qb, jc=jc: sc_mms(e, c, qb, jc),
                       reads=[r_q[c], r_k[c]], writes=[r_pb[sbk], r_pb[sbk + 1]], cost=450)

            def emit_av(pj):
                pp = pj % 4
                b.emit("tensor",
                       lambda e, c=c, pj=pj, pp=pp:
                           av_mms(e, c, pj, pp, pj == 0, pj == njc - 1),
                       reads=[r_v[pj], r_probs[pp]], writes=[r_pb[4], r_pb[5]],
                       cost=650)

            emit_sc(0)
            for jc in range(njc):
                par = jc % 4
                kd = jc - 4 * qb
                sbk = 2 * (jc % 2)
                if filler is not None and jc >= 1:
                    filler()
                if jc >= 3 and jc + 1 <= njc - 1:
                    pj = jc - 3
                    pp = pj % 4
                    nsbk = 2 * ((jc + 1) % 2)

                    def av_sc(eng, c=c, qb=qb, pj=pj, pp=pp, njc=njc, jc=jc):
                        sc_mms(eng, c, qb, jc + 1)
                        return av_mms(eng, c, pj, pp, pj == 0, pj == njc - 1)
                    b.emit("tensor", av_sc,
                           reads=[r_v[pj], r_probs[pp], r_q[c], r_k[c]],
                           writes=[r_pb[4], r_pb[5], r_pb[nsbk], r_pb[nsbk + 1]],
                           cost=1100)
                else:
                    if jc + 1 <= njc - 1:
                        emit_sc(jc + 1)
                    if jc >= 3:
                        emit_av(jc - 3)

                if kd > 0:
                    def exps(eng, par=par, sbk=sbk, kd=kd):
                        eng.activation(
                            AP(probs, par * 1024 + kd * 128,
                               [[4096, 128], [1, 512 - kd * 128]]),
                            AP(ps, sbk * 512 + kd * 128,
                               [[4096, 128], [1, 512 - kd * 128]]),
                            AF.Exp, bias=neg16[:, 0:1], scale=0.125)
                        return eng.activation(
                            AP(probs, par * 1024 + 512 + kd * 128,
                               [[4096, 128], [1, 512 - kd * 128]]),
                            AP(ps, (sbk + 1) * 512 + kd * 128,
                               [[4096, 128], [1, 512 - kd * 128]]),
                            AF.Exp, bias=neg16[:, 0:1], scale=0.125)
                    b.emit("scalar", exps,
                           reads=[r_pb[sbk], r_pb[sbk + 1], r_const],
                           writes=[r_probs[par]], cost=2 * (170 + 512 - kd * 128))
                else:
                    b.emit("scalar", lambda e, par=par, sbk=sbk: e.activation(
                        probs[:, par * 1024:(par + 1) * 1024],
                        ps[:, sbk * 512:(sbk + 2) * 512],
                        AF.Exp, bias=neg16[:, 0:1], scale=0.125),
                        reads=[r_pb[sbk], r_pb[sbk + 1], r_const],
                        writes=[r_probs[par]], cost=1100)

                if kd >= 0:
                    def masks(eng, par=par, kd=kd):
                        base = par * 1024
                        inst = None
                        for eo in range(2):
                            o = base + eo * 512 + kd * 128
                            inst = eng.tensor_tensor(
                                AP(probs, o, [[4096, 128], [1, 128]]),
                                AP(probs, o, [[4096, 128], [1, 128]]),
                                tri_sb[:, :], ALU.mult)
                        return inst
                    b.emit("vector", masks, reads=[r_const], writes=[r_probs[par]], cost=400)

                if jc == 0:
                    while pend:
                        pend.pop(0)()


            if njc >= 3:
                emit_av(njc - 3)
            if njc >= 2:
                emit_av(njc - 2)
            emit_av(njc - 1)

            xh = x1h(qb)
            rb = xs * 1024

            def emit_head(c=c, xs=xs, rb=rb, xh=xh, qb=qb):
                def evac(eng, c=c, xh=xh):
                    eng.tensor_copy(
                        AP(x1d, xh + c * 512, [[8192, 64], [1, 512]]),
                        ps[0:64, 4 * 512:5 * 512])
                    return eng.tensor_copy(
                        AP(x1d, 64 * 8192 + xh + c * 512, [[8192, 64], [1, 512]]),
                        ps[0:64, 5 * 512:6 * 512])
                b.emit("vector", evac, reads=[r_pb[4], r_pb[5]],
                       writes=r_x1[qb % 2][c], cost=1400)

                def lns(eng, rb=rb):
                    eng.activation(rowa[64:65, rb:rb + 512],
                                   ps[64:65, 4 * 512:5 * 512], AF.Ln)
                    return eng.activation(rowa[64:65, rb + 512:rb + 1024],
                                          ps[64:65, 5 * 512:6 * 512], AF.Ln)
                b.emit("scalar", lns, reads=[r_pb[4], r_pb[5]],
                       writes=[r_rowa[2 * xs], r_rowa[2 * xs + 1]], cost=1300)
                b.emit("scalar", lambda e, rb=rb: e.activation(
                    rowa[64:65, rb:rb + 1024], rowa[64:65, rb:rb + 1024],
                    AF.Exp, scale=-1.0),
                    reads=[r_rowa[2 * xs], r_rowa[2 * xs + 1]],
                    writes=[r_rowa[2 * xs], r_rowa[2 * xs + 1]])

            def emit_bc(xs=xs, rb=rb):
                if final_tail:
                    def bc_mm(eng, rb=rb):
                        eng.matmul(ps[0:64, 2 * 512:3 * 512],
                                   AP(ones_f, 64 * 128, [[128, 1], [1, 64]]),
                                   AP(rowa, 64 * 2048 + rb, [[2048, 1], [1, 512]]),
                                   start=True, stop=True, tile_position=(64, 0))
                        return eng.matmul(ps[64:128, 2 * 512:3 * 512],
                                          AP(ones_f, 64 * 128 + 64, [[128, 1], [1, 64]]),
                                          AP(rowa, 64 * 2048 + rb + 512,
                                             [[2048, 1], [1, 512]]),
                                          start=True, stop=True,
                                          tile_position=(64, 64))
                    b.emit("tensor", bc_mm,
                           reads=[r_rowa[2 * xs], r_rowa[2 * xs + 1], r_const],
                           writes=[r_pb[2]], cost=1900)
                    return
                b.emit("sync", lambda e, rb=rb, xs=xs: e.dma_start(
                    out=AP(bcscr_d, xs * 1024, [[1024, 1], [1, 1024]]),
                    in_=AP(rowa, 64 * 2048 + rb, [[2048, 1], [1, 1024]])),
                    reads=[r_rowa[2 * xs], r_rowa[2 * xs + 1]], writes=[r_scr[xs]])
                b.emit("sync", lambda e, xs=xs: e.dma_start(
                    out=AP(bc_sb, xs * 512, [[1024, 64], [1, 512]]),
                    in_=AP(bcscr_d, xs * 1024, [[0, 64], [1, 512]])),
                    reads=[r_scr[xs]], writes=[r_bcs[xs]])

            def emit_tail(c=c, xs=xs, xh=xh, qb=qb):
                if final_tail:
                    b.emit("vector", lambda e, c=c, xh=xh: e.tensor_tensor(
                        x1d[:, xh + c * 512:xh + (c + 1) * 512],
                        x1d[:, xh + c * 512:xh + (c + 1) * 512],
                        bank(2), ALU.mult),
                        reads=[r_pb[2]], writes=r_x1[qb % 2][c])
                else:
                    b.emit("sync", lambda e, xs=xs: e.dma_start(
                        out=AP(bc_sb, 64 * 1024 + xs * 512, [[1024, 64], [1, 512]]),
                        in_=AP(bcscr_d, xs * 1024 + 512, [[0, 64], [1, 512]])),
                        reads=[r_scr[xs]], writes=[r_bcs[xs]])
                    b.emit("vector", lambda e, c=c, xh=xh, xs=xs: e.tensor_tensor(
                        x1d[:, xh + c * 512:xh + (c + 1) * 512],
                        x1d[:, xh + c * 512:xh + (c + 1) * 512],
                        bc_sb[:, xs * 512:(xs + 1) * 512], ALU.mult),
                        reads=[r_bcs[xs]], writes=r_x1[qb % 2][c])
                b.emit("vector", lambda e, c=c, xh=xh, xs=xs: e.tensor_tensor(
                    x1d[:, xh + c * 512:xh + (c + 1) * 512],
                    x1d[:, xh + c * 512:xh + (c + 1) * 512],
                    xres[:, xs * 512:(xs + 1) * 512], ALU.add),
                    reads=[r_xres[xs]], writes=r_x1[qb % 2][c])
            pend.append(emit_head)
            pend.append(emit_bc)
            pend.append(emit_tail)

        def uu(lst, u):
            return list(lst) if u is None else [lst[u]]

        def x1all(qbb, u):
            return [r for ch in r_x1[qbb % 2] for r in uu(ch, u)]

        def ln_stats(qbb, s):
            xh = x1h(qbb)
            t0, tw, u, bm, br = s["t0"], s["tw"], s["u"], s["bm"], s["br"]
            split = qbb == NQB - 1

            def rbres(k, u=u):
                return [r_rowb[0][k], r_rowb[1][k]] if u is None else [r_rowb[u][k]]

            def x1some(lo, hi, u=u):
                return [r for ch in r_x1[qbb % 2][lo:hi] for r in uu(ch, u)]

            def cast_g(lo, hi):
                n = hi - lo
                b.emit("vector", lambda e, xh=xh, t0=t0, tw=tw, lo=lo, n=n:
                       e.tensor_copy(
                           AP(ka_sb, lo * 512 + t0, [[4096, 128], [512, n], [1, tw]]),
                           AP(x1d, xh + lo * 512 + t0, [[8192, 128], [512, n], [1, tw]])),
                       reads=x1some(lo, hi), writes=uu(r_kas, u),
                       cost=2400 * tw * n // (512 * 8) + 100)

            def mm_g(eng, bank, src, lo, hi, t0=t0, tw=tw):
                for cc in range(lo, hi):
                    inst = eng.matmul(ps[0:1, bank * 512 + t0:bank * 512 + t0 + tw],
                                      ones_b[:, 0:1],
                                      src[:, cc * 512 + t0:cc * 512 + t0 + tw],
                                      start=(cc == 0), stop=(cc == NCH - 1))
                return inst

            if split:
                for lo, hi in ((0, 7), (7, 8)):
                    cast_g(lo, hi)
                    b.emit("tensor",
                           lambda e, lo=lo, hi=hi: mm_g(e, bm, ka_sb, lo, hi),
                           reads=uu(r_kas, u) + [r_const], writes=[r_pb[bm]],
                           cost=(hi - lo) * 220 + 150)

                    def sq_g(eng, lo=lo, hi=hi, xh=xh, t0=t0, tw=tw):
                        for cc in range(lo, hi):
                            inst = eng.activation(
                                probs[:, cc * 512 + t0:cc * 512 + t0 + tw],
                                x1d[:, xh + cc * 512 + t0:xh + cc * 512 + t0 + tw],
                                AF.Square)
                        return inst
                    b.emit("scalar", sq_g,
                           reads=x1some(lo, hi), writes=list(r_probs),
                           cost=(hi - lo) * (150 + tw) + 100)
                    b.emit("tensor",
                           lambda e, lo=lo, hi=hi: mm_g(e, br, probs, lo, hi),
                           reads=list(r_probs) + [r_const], writes=[r_pb[br]],
                           cost=(hi - lo) * 220 + 150)
            else:
                cast_g(0, 8)
                b.emit("tensor", lambda e: mm_g(e, bm, ka_sb, 0, 8),
                       reads=uu(r_kas, u) + [r_const],
                       writes=[r_pb[bm]], cost=1600 * tw // 512 + 300)
                b.emit("vector", lambda e, t0=t0, tw=tw: e.tensor_tensor(
                    AP(ka_sb, t0, [[4096, 128], [512, 8], [1, tw]]),
                    AP(ka_sb, t0, [[4096, 128], [512, 8], [1, tw]]),
                    AP(ka_sb, t0, [[4096, 128], [512, 8], [1, tw]]), ALU.mult),
                    reads=uu(r_kas, u), writes=uu(r_kas, u), cost=2400 * tw // 512)
                b.emit("tensor", lambda e: mm_g(e, br, ka_sb, 0, 8),
                       reads=uu(r_kas, u) + [r_const],
                       writes=[r_pb[br]], cost=1600 * tw // 512 + 300)

            if split:
                def rows1_s(eng, bm=bm, br=br, t0=t0, tw=tw):
                    eng.activation(rowb[0:1, t0:t0 + tw],
                                   ps[0:1, bm * 512 + t0:bm * 512 + t0 + tw],
                                   AF.Copy, scale=1.0 / D)
                    return eng.activation(
                        rowb[0:1, 512 + t0:512 + t0 + tw],
                        ps[0:1, br * 512 + t0:br * 512 + t0 + tw],
                        AF.Copy, scale=1.0 / D)
                b.emit("scalar", rows1_s, reads=[r_pb[bm], r_pb[br]],
                       writes=rbres(0) + rbres(1))
                b.emit("gpsimd", lambda e, t0=t0, tw=tw: e.tensor_tensor(
                    rowb[0:1, 1024 + t0:1024 + t0 + tw], rowb[0:1, t0:t0 + tw],
                    rowb[0:1, t0:t0 + tw], ALU.mult),
                    reads=rbres(0), writes=rbres(2), cost=400)
                b.emit("gpsimd", lambda e, t0=t0, tw=tw: e.tensor_tensor(
                    rowb[0:1, 1024 + t0:1024 + t0 + tw],
                    rowb[0:1, 512 + t0:512 + t0 + tw],
                    rowb[0:1, 1024 + t0:1024 + t0 + tw], ALU.subtract),
                    reads=rbres(1) + rbres(2), writes=rbres(2), cost=400)
            else:
                def rows1(eng, bm=bm, br=br, t0=t0, tw=tw):
                    eng.tensor_scalar(rowb[0:1, t0:t0 + tw],
                                      ps[0:1, bm * 512 + t0:bm * 512 + t0 + tw],
                                      1.0 / D, None, ALU.mult)
                    return eng.tensor_scalar(
                        rowb[0:1, 512 + t0:512 + t0 + tw],
                        ps[0:1, br * 512 + t0:br * 512 + t0 + tw],
                        1.0 / D, None, ALU.mult)
                b.emit("vector", rows1, reads=[r_pb[bm], r_pb[br]],
                       writes=rbres(0) + rbres(1))
                b.emit("vector", lambda e, t0=t0, tw=tw: e.tensor_tensor(
                    rowb[0:1, 1024 + t0:1024 + t0 + tw], rowb[0:1, t0:t0 + tw],
                    rowb[0:1, t0:t0 + tw], ALU.mult),
                    reads=rbres(0), writes=rbres(2))
                b.emit("vector", lambda e, t0=t0, tw=tw: e.tensor_tensor(
                    rowb[0:1, 1024 + t0:1024 + t0 + tw],
                    rowb[0:1, 512 + t0:512 + t0 + tw],
                    rowb[0:1, 1024 + t0:1024 + t0 + tw], ALU.subtract),
                    reads=rbres(1) + rbres(2), writes=rbres(2))
            b.emit("scalar", lambda e, t0=t0, tw=tw: e.activation(
                rowb[0:1, 1024 + t0:1024 + t0 + tw],
                rowb[0:1, 1024 + t0:1024 + t0 + tw], AF.Ln,
                bias=epsc[0:1, 0:1]),
                reads=rbres(2) + [r_const], writes=rbres(3))
            b.emit("scalar", lambda e, t0=t0, tw=tw: e.activation(
                rowb[0:1, 1024 + t0:1024 + t0 + tw],
                rowb[0:1, 1024 + t0:1024 + t0 + tw], AF.Exp, scale=-0.5),
                reads=rbres(3), writes=rbres(3))

            def bc2(eng, bm=bm, br=br, t0=t0, tw=tw):
                eng.matmul(ps[:, bm * 512 + t0:bm * 512 + t0 + tw],
                           ones_f[0:1, 0:128],
                           rowb[0:1, t0:t0 + tw], start=True, stop=True)
                return eng.matmul(ps[:, br * 512 + t0:br * 512 + t0 + tw],
                                  ones_f[0:1, 0:128],
                                  AP(rowb, 1024 + t0, [[1536, 1], [1, tw]]),
                                  start=True, stop=True)
            b.emit("tensor", bc2, reads=rbres(0) + rbres(3) + [r_const],
                   writes=[r_pb[bm], r_pb[br]], cost=1900 * tw // 512)

        def piece_ln1(qbb, s):
            xh = x1h(qbb)
            t0, tw, u, bm, br = s["t0"], s["tw"], s["u"], s["bm"], s["br"]
            ln_stats(qbb, s)

            def ln_sub(eng, xh=xh, t0=t0, tw=tw, bm=bm):
                for cc in range(NCH):
                    co = xh + cc * 512 + t0
                    inst = eng.tensor_tensor(
                        x1d[:, co:co + tw], x1d[:, co:co + tw],
                        ps[:, bm * 512 + t0:bm * 512 + t0 + tw], ALU.subtract)
                return inst
            skew = [r_xrs[0]] if u == 1 else []
            b.emit("vector", ln_sub, reads=[r_pb[bm]] + skew, writes=x1all(qbb, u),
                   cost=5400 * tw // 512)

            def ln_mul(eng, xh=xh, t0=t0, tw=tw, br=br):
                for cc in range(NCH):
                    co = xh + cc * 512 + t0
                    inst = eng.tensor_tensor(
                        x1d[:, co:co + tw], x1d[:, co:co + tw],
                        ps[:, br * 512 + t0:br * 512 + t0 + tw], ALU.mult)
                return inst
            b.emit("vector", ln_mul, reads=[r_pb[br]], writes=x1all(qbb, u),
                   cost=5400 * tw // 512)

            if qbb < NQB - 1:
                hw = qbb % 2
                b.emit("vector", lambda e, hw=hw, xh=xh: e.tensor_copy(
                    halo[:, hw * 8:hw * 8 + 8],
                    AP(x1d, xh + 511, [[8192, 128], [512, 8]])),
                    reads=x1all(qbb, u), writes=[r_halo[hw]])

            hr = (qbb + 1) % 2

            def shift_a(eng, dst, omu, hr, xh, t0=t0, tw=tw):
                for cc in range(NCH):
                    co = cc * 512 + t0
                    eng.tensor_scalar(dst[:, co + 1:co + tw],
                                      x1d[:, xh + co:xh + co + tw - 1],
                                      omu[:, cc:cc + 1], None, ALU.mult)
                    if t0 == 0:
                        src = halo[:, hr * 8 + cc:hr * 8 + cc + 1]
                    else:
                        src = x1d[:, xh + co - 1:xh + co]
                    inst = eng.tensor_scalar(dst[:, co:co + 1], src,
                                             omu[:, cc:cc + 1], None, ALU.mult)
                return inst

            def shift_b(eng, dst, mu, xh, t0=t0, tw=tw):
                for cc in range(NCH):
                    co = cc * 512 + t0
                    inst = eng.scalar_tensor_tensor(
                        dst[:, co:co + tw], x1d[:, xh + co:xh + co + tw],
                        mu[:, cc:cc + 1], dst[:, co:co + tw], ALU.mult, ALU.add)
                return inst

            reads_a = x1all(qbb, u) + [r_const]
            if t0 == 0:
                reads_a.append(r_halo[hr])
            else:
                reads_a += x1all(qbb, 0)
            for dst, mu, omu, rr in ((xk_sb, muk_sb, omuk_sb, r_xks),
                                     (xr_sb, mur_sb, omur_sb, r_xrs)):
                b.emit("vector",
                       lambda e, dst=dst, omu=omu, hr=hr, xh=xh:
                           shift_a(e, dst, omu, hr, xh),
                       reads=reads_a, writes=uu(rr, u), cost=4600 * tw // 512)
                b.emit("vector",
                       lambda e, dst=dst, mu=mu, xh=xh: shift_b(e, dst, mu, xh),
                       reads=x1all(qbb, u) + [r_const], writes=uu(rr, u),
                       cost=4900 * tw // 512)

        def wslab(qbb, dram_f32, dram_b16, dst, rslot, rwb, idx, sl):
            if qbb == 0:
                b.emit("gpsimd", lambda e: e.dma_start(
                    out=dst[:, sl * 1024:(sl + 1) * 1024],
                    in_=AP(dram_f32, idx * 131072, [[1024, 128], [1, 1024]])),
                    writes=[rslot])
                b.emit("gpsimd", lambda e: e.dma_start(
                    out=AP(dram_b16, idx * 131072, [[1024, 128], [1, 1024]]),
                    in_=dst[:, sl * 1024:(sl + 1) * 1024]),
                    reads=[rslot], writes=[rwb])
            else:
                b.emit("gpsimd", lambda e: e.dma_start(
                    out=dst[:, sl * 1024:(sl + 1) * 1024],
                    in_=AP(dram_b16, idx * 131072, [[1024, 128], [1, 1024]])),
                    reads=[rwb], writes=[rslot])

        def piece_3b(qbb, half, subs):
            for dq in range(4 * half, 4 * half + 4):
                sl = dq % 2
                wslab(qbb, wk2, wk2b_d, wk2_b, r_wk2b[sl], r_wbk2, dq, sl)
                for s in subs:
                    t0, tw, u = s["t0"], s["tw"], s["u"]
                    bk = s["kb"][sl]

                    def ka_mms(eng, dq=dq, sl=sl, bk=bk, t0=t0, tw=tw):
                        for cc in range(NCH):
                            inst = eng.matmul(
                                ps[:, bk * 512 + t0:bk * 512 + t0 + tw],
                                wk2_b[:, sl * 1024 + cc * 128:sl * 1024 + (cc + 1) * 128],
                                xk_sb[:, cc * 512 + t0:cc * 512 + t0 + tw],
                                start=(cc == 0), stop=(cc == NCH - 1))
                        return inst
                    b.emit("tensor", ka_mms, reads=[r_wk2b[sl]] + uu(r_xks, u),
                           writes=[r_pb[bk]], cost=1700 * tw // 512 + 200)

                    b.emit("scalar", lambda e, sl=sl, bk=bk, t0=t0, tw=tw: e.activation(
                        tmp512[:, sl * 512 + t0:sl * 512 + t0 + tw],
                        ps[:, bk * 512 + t0:bk * 512 + t0 + tw], AF.Relu),
                        reads=[r_pb[bk]], writes=uu(r_t512[sl], u),
                        cost=600 * tw // 512 + 150)

                    b.emit("vector", lambda e, dq=dq, sl=sl, t0=t0, tw=tw: e.tensor_tensor(
                        ka_sb[:, dq * 512 + t0:dq * 512 + t0 + tw],
                        tmp512[:, sl * 512 + t0:sl * 512 + t0 + tw],
                        tmp512[:, sl * 512 + t0:sl * 512 + t0 + tw], ALU.mult),
                        reads=uu(r_t512[sl], u), writes=uu(r_kas, u),
                        cost=600 * tw // 512 + 100)

        def piece_3c(qbb, quarter, subs):
            xh = x1h(qbb)
            for dyq in range(2 * quarter, 2 * quarter + 2):
                sl = dyq % 2
                wslab(qbb, wr, wrb_d, wr_b, r_wrb[sl], r_wbr, dyq, sl)
                wslab(qbb, wv2, wv2b_d, wv2_b, r_wv2b[sl], r_wbv2, dyq, sl)
                for s in subs:
                    t0, tw, u = s["t0"], s["tw"], s["u"]
                    ybk = s["yb"]
                    rbk = s["rb"]

                    def y_mms(eng, sl=sl, ybk=ybk, t0=t0, tw=tw):
                        for dq in range(NCH):
                            inst = eng.matmul(
                                ps[:, ybk * 512 + t0:ybk * 512 + t0 + tw],
                                wv2_b[:, sl * 1024 + dq * 128:sl * 1024 + (dq + 1) * 128],
                                ka_sb[:, dq * 512 + t0:dq * 512 + t0 + tw],
                                start=(dq == 0), stop=(dq == NCH - 1))
                        return inst
                    b.emit("tensor", y_mms, reads=[r_wv2b[sl]] + uu(r_kas, u),
                           writes=[r_pb[ybk]], cost=1700 * tw // 512 + 200)

                    def r_mms(eng, sl=sl, rbk=rbk, t0=t0, tw=tw):
                        for cc in range(NCH):
                            inst = eng.matmul(
                                ps[:, rbk * 512 + t0:rbk * 512 + t0 + tw],
                                wr_b[:, sl * 1024 + cc * 128:sl * 1024 + (cc + 1) * 128],
                                xr_sb[:, cc * 512 + t0:cc * 512 + t0 + tw],
                                start=(cc == 0), stop=(cc == NCH - 1))
                        return inst
                    b.emit("tensor", r_mms, reads=[r_wrb[sl]] + uu(r_xrs, u),
                           writes=[r_pb[rbk]], cost=1700 * tw // 512 + 200)

                    b.emit("scalar", lambda e, sl=sl, rbk=rbk, t0=t0, tw=tw: e.activation(
                        rout[:, sl * 512 + t0:sl * 512 + t0 + tw],
                        ps[:, rbk * 512 + t0:rbk * 512 + t0 + tw], AF.Sigmoid),
                        reads=[r_pb[rbk]], writes=uu(r_rout[sl], u),
                        cost=600 * tw // 512 + 150)
                    b.emit("vector", lambda e, sl=sl, ybk=ybk, t0=t0, tw=tw: e.tensor_tensor(
                        tmp512[:, sl * 512 + t0:sl * 512 + t0 + tw],
                        rout[:, sl * 512 + t0:sl * 512 + t0 + tw],
                        ps[:, ybk * 512 + t0:ybk * 512 + t0 + tw], ALU.mult),
                        reads=uu(r_rout[sl], u) + [r_pb[ybk]], writes=uu(r_t512[sl], u),
                        cost=600 * tw // 512 + 100)
                    b.emit("vector", lambda e, dyq=dyq, sl=sl, xh=xh, t0=t0, tw=tw:
                           e.tensor_tensor(
                        x1d[:, xh + dyq * 512 + t0:xh + dyq * 512 + t0 + tw],
                        x1d[:, xh + dyq * 512 + t0:xh + dyq * 512 + t0 + tw],
                        tmp512[:, sl * 512 + t0:sl * 512 + t0 + tw], ALU.add),
                        reads=uu(r_t512[sl], u), writes=uu(r_x1[qbb % 2][dyq], u),
                        cost=600 * tw // 512 + 100)

        def piece_ln2(qbb, s):
            xh = x1h(qbb)
            t0, tw, u, bm, br = s["t0"], s["tw"], s["u"], s["bm"], s["br"]
            ln_stats(qbb, s)
            for dyq in range(NCH):
                sl = dyq % 2
                b.emit("vector", lambda e, dyq=dyq, sl=sl, xh=xh, t0=t0, tw=tw, bm=bm:
                       e.tensor_tensor(
                    rout[:, sl * 512 + t0:sl * 512 + t0 + tw],
                    x1d[:, xh + dyq * 512 + t0:xh + dyq * 512 + t0 + tw],
                    ps[:, bm * 512 + t0:bm * 512 + t0 + tw], ALU.subtract),
                    reads=uu(r_x1[qbb % 2][dyq], u) + [r_pb[bm]],
                    writes=uu(r_rout[sl], u), cost=600 * tw // 512 + 100)
                b.emit("vector", lambda e, sl=sl, t0=t0, tw=tw, br=br: e.tensor_tensor(
                    rout[:, sl * 512 + t0:sl * 512 + t0 + tw],
                    rout[:, sl * 512 + t0:sl * 512 + t0 + tw],
                    ps[:, br * 512 + t0:br * 512 + t0 + tw], ALU.mult),
                    reads=[r_pb[br]], writes=uu(r_rout[sl], u),
                    cost=600 * tw // 512 + 100)
                b.emit("sync", lambda e, dyq=dyq, sl=sl, qbb=qbb, t0=t0, tw=tw: e.dma_start(
                    out=AP(yT, dyq * 128 * 2048 + qbb * 512 + t0, [[2048, 128], [1, tw]]),
                    in_=rout[:, sl * 512 + t0:sl * 512 + t0 + tw]),
                    reads=uu(r_rout[sl], u), writes=[r_yT[sl]],
                    cost=1300 * tw // 512)

        SUB_FULL = dict(t0=0, tw=512, u=None, bm=6, br=7, kb=(6, 7), yb=6, rb=7)
        SUB_A = dict(t0=0, tw=256, u=0, bm=6, br=7, kb=(6, 7), yb=6, rb=7)
        SUB_B = dict(t0=256, tw=256, u=1, bm=0, br=1, kb=(0, 1), yb=0, rb=1)

        def b_piece(qbb, i):
            subs = [SUB_A, SUB_B] if qbb == NQB - 1 else [SUB_FULL]
            if i == 0:
                for s in subs:
                    piece_ln1(qbb, s)
            elif i in (1, 2):
                piece_3b(qbb, i - 1, subs)
            elif i in (3, 4, 5, 6):
                piece_3c(qbb, i - 3, subs)
            else:
                for s in subs:
                    piece_ln2(qbb, s)

        with nc.sbuf_tensor("x_bf", [128, 16384], BF16) as x_bf, \
             nc.sbuf_tensor("wqkb", [128, 2048], BF16) as wqkb, \
             nc.sbuf_tensor("wvb", [128, 8192], BF16) as wvb:
            r_xbf = [_Res() for _ in range(4)]
            r_wqkb = [_Res() for _ in range(2)]
            r_wvb = [_Res() for _ in range(2)]

            for tb in range(4):
                b.emit("gpsimd", lambda e, tb=tb: e.dma_start(
                    out=AP(x_bf, tb * 512, [[16384, 128], [2048, 8], [1, 512]]),
                    in_=AP(xT, tb * 512, [[2048, 128], [128 * 2048, 8], [1, 512]])),
                    writes=[r_xbf[tb]], cost=6000)

            for dh in range(2):
                sl = dh % 2
                b.emit("gpsimd", lambda e, dh=dh, sl=sl: e.dma_start(
                    out=wvb[:, sl * 4096:(sl + 1) * 4096],
                    in_=AP(wv, dh * 128 * 4096, [[4096, 128], [1, 4096]])),
                    writes=[r_wvb[sl]], cost=5600)
                for tt in range(NJT):
                    bk = 6 + (dh * NJT + tt) % 2

                    def v_mms(eng, sl=sl, tt=tt, bk=bk):
                        for cc in range(NCH):
                            inst = eng.matmul(
                                bank(bk),
                                AP(x_bf, cc * 2048 + tt * 128, [[16384, 128], [1, 128]]),
                                AP(wvb, sl * 4096 + cc * 512, [[8192, 128], [1, 512]]),
                                start=(cc == 0), stop=(cc == NCH - 1))
                        return inst
                    b.emit("tensor", v_mms, reads=[r_xbf[tt // 4], r_wvb[sl]],
                           writes=[r_pb[bk]], cost=1900)

                    b.emit("vector", lambda e, dh=dh, tt=tt, bk=bk: e.tensor_copy(
                        AP(v_sb, tt * VROW + dh * 8 * VW,
                           [[VROW * NJT, 128], [VW, 8], [1, 64]]),
                        bank(bk)),
                        reads=[r_pb[bk]], writes=[r_v[tt]])

            for c in range(NCH):
                for half in range(2):
                    dqk = c + 8 * half
                    sl = half
                    b.emit("gpsimd", lambda e, dqk=dqk, sl=sl: e.dma_start(
                        out=wqkb[:, sl * 1024:(sl + 1) * 1024],
                        in_=AP(wqk, dqk * 128 * 1024, [[1024, 128], [1, 1024]])),
                        writes=[r_wqkb[sl]], cost=1900)
                    dst = q_sb if half == 0 else k_sb
                    rdst = r_q if half == 0 else r_k
                    for tb in range(4):
                        bk = 6 + (dqk * 4 + tb) % 2

                        def qk_mms(eng, sl=sl, tb=tb, bk=bk):
                            for cc in range(NCH):
                                inst = eng.matmul(
                                    bank(bk),
                                    AP(wqkb, sl * 1024 + cc * 128,
                                       [[2048, 128], [1, 128]]),
                                    AP(x_bf, cc * 2048 + tb * 512,
                                       [[16384, 128], [1, 512]]),
                                    start=(cc == 0), stop=(cc == NCH - 1))
                            return inst
                        b.emit("tensor", qk_mms, reads=[r_xbf[tb], r_wqkb[sl]],
                               writes=[r_pb[bk]], cost=1900)

                        b.emit("scalar", lambda e, dst=dst, c=c, tb=tb, bk=bk: e.copy(
                            dst[:, c * 2048 + tb * 512:c * 2048 + (tb + 1) * 512],
                            bank(bk)),
                            reads=[r_pb[bk]], writes=[rdst[c]])

                attn_pair(0, c)

        r_p1bar = _Res()
        b.emit("vector", lambda e: e.nop(),
               writes=[r_xbf[0], r_xbf[1], r_xbf[2], r_xbf[3],
                       r_wqkb[0], r_wqkb[1], r_wvb[0], r_wvb[1], r_p1bar])

        rowb = sb("rowb", [128, 1536], F32)
        ka_sb = sb("ka_sb", [128, 4096], BF16)
        xr_sb = sb("xr_sb", [128, 4096], BF16)
        xk_sb = sb("xk_sb", [128, 4096], BF16)
        tmp512 = sb("tmp512", [128, 1024], F32)
        rout = sb("rout", [128, 1024], F32)
        wr_b = sb("wr_b", [128, 2048], BF16)
        wv2_b = sb("wv2_b", [128, 2048], BF16)
        wk2_b = sb("wk2_b", [128, 2048], BF16)

        r_kas = [_Res(), _Res()]
        r_xrs = [_Res(), _Res()]
        r_xks = [_Res(), _Res()]
        r_t512 = [[_Res(), _Res()] for _ in range(2)]
        r_rout = [[_Res(), _Res()] for _ in range(2)]
        r_wrb = [_Res() for _ in range(2)]
        r_wv2b = [_Res() for _ in range(2)]
        r_wk2b = [_Res() for _ in range(2)]
        r_yT = [_Res() for _ in range(2)]
        r_wbr = _Res()
        r_wbv2 = _Res()
        r_wbk2 = _Res()

        b.ambient = [r_p1bar]

        for qb in range(1, NQB + 1):
            b.buf = []
            for i in range(NCH):
                b_piece(qb - 1, i)
            blist = b.buf
            b.buf = None
            if qb == NQB:
                while pend:
                    pend.pop(0)()
                for it in blist:
                    b.emit(*it)
                break
            slots = [NCH * (4 * qb + 4)]

            def filler(blist=blist, slots=slots):
                if not blist:
                    return
                k = max(1, -(-len(blist) // max(1, slots[0])))
                for _ in range(min(k, len(blist))):
                    b.emit(*blist.pop(0))
                slots[0] -= 1
            for i in range(NCH):
                attn_pair(qb, i, filler)
            while blist:
                b.emit(*blist.pop(0))

        order, est_fin = b.schedule()

        b.finalize(order)
        sems = {n: ctx.enter_context(nc.semaphore("sem_" + n))
                for n in ("sync", "gpsimd", "tensor", "vector", "scalar")}
        for sn in b.stream_names:
            sems[sn] = ctx.enter_context(nc.semaphore(sn))

        finals = {}
        for qname in ("sync", "gpsimd"):
            seen = {}
            for _, _, (sem_name, val) in b.engs[qname].items:
                seen[sem_name] = max(seen.get(sem_name, 0), val)
            finals[qname] = sorted(seen.items())

        block = ctx.enter_context(nc.Block())

        def mk(name):
            eng_obj = b.engs[name]

            def body(eng, items=eng_obj.items, name=name):
                for waits, fn, (sem_name, val) in items:
                    for (snm, v) in waits:
                        eng.wait_ge(sems[snm], v)
                    inst = fn(eng)
                    inst.then_inc(sems[sem_name], 16 if name not in _COMPUTE else 1)
                for sem_name, val in finals.get(name, ()):
                    eng.wait_ge(sems[sem_name], val)
            return body

        block.sync(mk("sync"))
        block.gpsimd(mk("gpsimd"))
        block.tensor(mk("tensor"))
        block.vector(mk("vector"))
        block.scalar(mk("scalar"))

    return nc


def _stage_inputs(x, W_qkv, W_r, W_k, W_v, mu_r, mu_k):
    W_qkv = np.ascontiguousarray(np.asarray(W_qkv, dtype=np.float32))
    wqk = np.ascontiguousarray(
        W_qkv[:, :2048].reshape(8, 128, 16, 128).transpose(2, 1, 0, 3).reshape(16, 128, 1024))
    wv = np.ascontiguousarray(
        W_qkv[:, 2048:].reshape(8, 128, 2, 512).transpose(2, 1, 0, 3).reshape(2, 128, 4096))

    def stage_ffn(W):
        W = np.asarray(W, dtype=np.float32)
        return np.ascontiguousarray(
            W.reshape(8, 128, 8, 128).transpose(0, 3, 2, 1).reshape(8, 128, 1024))

    wr = stage_ffn(W_r)
    wk2 = stage_ffn(W_k)
    wv2 = stage_ffn(W_v)
    mur = np.ascontiguousarray(np.asarray(mu_r, dtype=np.float32).reshape(8, 128).T)
    muk = np.ascontiguousarray(np.asarray(mu_k, dtype=np.float32).reshape(8, 128).T)

    x = np.asarray(x, dtype=np.float32)
    in_maps = []
    for bidx in range(8):
        in_maps.append({
            "xT": np.ascontiguousarray(x[bidx].T),
            "wqk": wqk, "wv": wv, "wr": wr, "wk2": wk2, "wv2": wv2,
            "mur": mur, "muk": muk,
        })
    return in_maps


_NC_CACHE = {}


def kernel(L0_attn_x, L0_attn_W_qkv, L0_ffn_W_r, L0_ffn_W_k, L0_ffn_W_v,
           L0_ffn_mu_r, L0_ffn_mu_k):
    in_maps = _stage_inputs(L0_attn_x, L0_attn_W_qkv, L0_ffn_W_r, L0_ffn_W_k,
                            L0_ffn_W_v, L0_ffn_mu_r, L0_ffn_mu_k)
    key = "v6"
    if key not in _NC_CACHE:
        _NC_CACHE[key] = build_program()
    nc = _NC_CACHE[key]
    res = run_bass_kernel_spmd(nc, in_maps, core_ids=list(range(8)))
    return np.stack([np.ascontiguousarray(r["yT"].T) for r in res.results])
